# version 39
# speedup vs baseline: 1.3498x; 1.0025x over previous
import math
from contextlib import ExitStack

import numpy as np

import concourse.bass as bass
import concourse.mybir as mybir
from concourse.bass_utils import run_bass_kernel_spmd

F32 = mybir.dt.float32
BF16 = mybir.dt.bfloat16
AF = mybir.ActivationFunctionType
ALU = mybir.AluOpType
AX = mybir.AxisListType.X

B, S, D, FF = 4, 4096, 1024, 4096
TQ = S // 2
NDC = D // 128
NTB = S // 512
NTBQ = TQ // 512
NTT = S // 128
NTTQ = TQ // 128
NP = 8
MAGIC = float(1.5 * 2 ** 23)
EPS = 1e-5
VW = 65
VBLK = 16 * VW


def build_nc():
    nc = bass.Bass("TRN2", target_bir_lowering=False, debug=False)

    def din(name, shape):
        return nc.dram_tensor(name, shape, F32, kind="ExternalInput").ap()

    x_fm = din("x_fm", [D, S])
    x_fmq = din("x_fmq", [D, TQ])
    x_tmq = din("x_tmq", [TQ, D])
    wqT = din("wqT", [D, D])
    wkT = din("wkT", [D, D])
    wvT = din("wvT", [D, D])
    woT = din("woT", [D, D])
    w1T = din("w1T", [D, FF])
    w2T = din("w2T", [FF, D])
    b1r = din("b1r", [1, FF])
    b2r = din("b2r", [1, D])
    tfm = din("tfm", [128, S])
    tfmq = din("tfmq", [128, TQ])
    eyef = din("eyef", [128, 128])

    y_out = nc.dram_tensor("y", [TQ, D], F32, kind="ExternalOutput").ap()
    kscr = nc.dram_tensor("kscr", [NP, 128, S], BF16).ap()
    qscr = nc.dram_tensor("qscr", [NP, 128, TQ], BF16).ap()
    y1scr = nc.dram_tensor("y1scr", [TQ, D], F32).ap()

    st_all = ExitStack()
    with st_all:
        def mk(stack, name, shape, dtype=F32, side=None):
            kw = {} if side is None else {"side": side}
            return stack.enter_context(nc.sbuf_tensor(name, shape, dtype, **kw))

        def mkp(stack, name, shape, dtype=F32):
            return stack.enter_context(nc.psum_tensor(name, shape, dtype))

        SEMS = [
            "sBAR",
            "dW", "dWT", "dX0a", "dX0b", "dX0c", "dX0d", "dX0e", "dX0f", "dX0g", "dX0h",
            "dWWq", "dWWk", "dWWv", "vWCS", "vCV", "aSQ0", "pST", "vROW", "vRC0",
            "aRC0", "pBC", "vBC",
            "dX1a", "dX1b", "pXF", "pV", "vVE", "pK", "vKR", "aKS",
            "dSPa", "dSPb", "pN", "vNF",
            "dKQa", "dKQb", "pSC", "aEX", "pAV", "vRC", "pB2", "vRL", "vEV",
            "dW1a", "dW1b", "vW1", "pMS", "vMS",
            "dWO", "dWOb", "vCVo", "dXQa", "dXQb", "pY", "vY1", "dY1a", "dY1b",
            "dW2a", "dW2b", "dW2c", "vW2", "aW2",
            "dYLa", "dYLb", "dYLc", "vP4a", "aP4a", "pIC", "vIC", "vW2q", "dB",
            "vQ1", "pTE", "vTE", "vTEB", "vTEBD", "pL1", "aGE", "aSQ", "vSF", "aC2", "vC2",
            "vQ2", "vQ2c", "pL2", "vOU", "dOUa", "dOUb",
        ]
        SM = {n: st_all.enter_context(nc.semaphore(n)) for n in SEMS}

        eye_b = mk(st_all, "eye_b", [128, 128], BF16)
        ones_b = mk(st_all, "ones_b", [128, 1], BF16)
        ones_f = mk(st_all, "ones_f", [128, 1])
        ones_f1 = mk(st_all, "ones_f1", [1, 128])
        ones_b1 = mk(st_all, "ones_b1", [1, 128], BF16)
        r_tm = mk(st_all, "r_tm", [128, NTT])
        a1c = mk(st_all, "a1c", [128, 1])
        a2c = mk(st_all, "a2c", [128, 1])
        inva1 = mk(st_all, "inva1", [128, 1])
        inva2 = mk(st_all, "inva2", [128, 1])
        wsum1 = mk(st_all, "wsum1", [128, 1])
        wsum2 = mk(st_all, "wsum2", [128, 1])
        wtmp1 = mk(st_all, "wtmp1", [128, 1])
        a1row = mk(st_all, "a1row", [1, 2])
        epsv = mk(st_all, "epsv", [128, 1])
        mgv = mk(st_all, "mgv", [128, 1])
        t30v = mk(st_all, "t30v", [128, 1])
        scr4 = mk(st_all, "scr4", [128, 16])
        c1q_t = mk(st_all, "c1q_t", [128, NTTQ])
        kmag_t = mk(st_all, "kmag_t", [128, NTTQ])
        c1o_t = mk(st_all, "c1o_t", [128, NTTQ])

        C = {"s1": 0, "ssq": 1, "m2": 2, "v2": 3, "rho2": 4, "msu": 5,
             "rr": 6, "mx": 7, "mrc": 8, "t9": 9, "t10": 10, "ic": 11,
             "sqa": 12, "sqb": 13}

        def col(tile, name):
            i = C[name]
            return tile[:, i:i + 1]

        block = st_all.enter_context(nc.Block())
        bar_k = [0]

        def barrier_all():
            k = bar_k[0] + 1

            def mkbar(e):
                e.drain().then_inc(SM["sBAR"], 1)
                e.wait_ge(SM["sBAR"], 5 * k)

            for dec in (block.sync, block.gpsimd, block.vector,
                        block.scalar, block.tensor):
                dec(mkbar)
            bar_k[0] = k

        st_A = ExitStack()
        T_rk = mk(st_A, "T_rk", [128, S])
        T_rq = mk(st_A, "T_rq", [128, TQ])
        m_row_b = mk(st_A, "m_row_b", [1, S], BF16)
        mq_row_b = mk(st_A, "mq_row_b", [1, TQ], BF16)
        wqcs_n = mk(st_A, "wqcs_n", [1, D], BF16)
        wkcs_n = mk(st_A, "wkcs_n", [1, D], BF16)
        wvcs_n = mk(st_A, "wvcs_n", [1, D], BF16)
        wq_b = mk(st_A, "wq_b", [128, NDC * 1024], BF16)
        wk_b = mk(st_A, "wk_b", [128, NDC * 1024], BF16)
        wv_b = mk(st_A, "wv_b", [128, NDC * 1024], BF16)

        st0 = ExitStack()
        eye_f = mk(st0, "eye_f", [128, 128])
        onef11 = mk(st0, "onef11", [1, 1])
        xb = [mk(st0, f"xb{i}", [128, 512], BF16) for i in range(8)]
        xsq = [mk(st0, f"xsq{i}", [128, 512], BF16) for i in range(8)]
        m_f = mk(st0, "m_f", [1, S])
        s2r_f = mk(st0, "s2r_f", [1, S])
        rowtmp = mk(st0, "rowtmp", [1, S])

        ps0_m = mkp(st0, "ps0_m", [1, 512])
        ps0_v = mkp(st0, "ps0_v", [1, 512])
        ps0_bc = [mkp(st0, f"ps0_bc{i}", [128, 512]) for i in range(2)]
        ps0_w = mkp(st0, "ps0_w", [1, 1024])
        ps0_r1 = [mkp(st0, f"ps0_r1{i}", [128, 1]) for i in range(2)]

        X8 = ["dX0a", "dX0b", "dX0c", "dX0d", "dX0e", "dX0f", "dX0g", "dX0h"]
        WL = [(wqT, wq_b, "dWWq"), (wkT, wk_b, "dWWk"), (wvT, wv_b, "dWWv")]
        WLC = [(im, c) for im in range(3) for c in range(NDC)]

        @block.sync
        def _(e):
            e.dma_start(out=eye_f[:], in_=eyef).then_inc(SM["dW"], 16)
            e.dma_start(out=T_rk[:], in_=tfm).then_inc(SM["dWT"], 16)
            e.dma_start(out=T_rq[:], in_=tfmq).then_inc(SM["dWT"], 16)

        @block.gpsimd
        def _(e):
            for i in range(NTB):
                for c in range(NDC):
                    g = i * NDC + c
                    if g >= 8:
                        e.wait_ge(SM["pST"], g - 7)
                        e.wait_ge(SM["aSQ0"], g - 7)
                    e.dma_start(
                        out=xb[g % 8][:, :],
                        in_=x_fm[c * 128:(c + 1) * 128, i * 512:(i + 1) * 512],
                    ).then_inc(SM[X8[g % 8]], 16)
                for k3 in range(3):
                    im, wc = WLC[i * 3 + k3]
                    wsrc, wdst, sem = WL[im]
                    e.dma_start(
                        out=wdst[:, wc * 1024:(wc + 1) * 1024],
                        in_=wsrc[wc * 128:(wc + 1) * 128, :],
                    ).then_inc(SM[sem], 16)

        @block.scalar
        def _(e):
            def chain0(b):
                sl = slice(b * 512, (b + 1) * 512)
                e.wait_ge(SM["vRC0"], b + 1)
                e.activation(s2r_f.ap()[0:1, sl], s2r_f.ap()[0:1, sl],
                             AF.Ln, scale=float(1.0 / (D * D)),
                             bias=epsv[0:1, 0:1])
                e.drain()
                e.activation(s2r_f.ap()[0:1, sl], s2r_f.ap()[0:1, sl],
                             AF.Exp, scale=-0.5)
                e.drain().then_inc(SM["aRC0"], 1)

            for g in range(NTB * NDC):
                e.wait_ge(SM[X8[g % 8]], 16 * (g // 8 + 1))
                if g >= 8:
                    e.wait_ge(SM["pST"], g - 7)
                e.activation(xsq[g % 8][:, :], xb[g % 8][:, :],
                             AF.Square).then_inc(SM["aSQ0"], 1)
                if g % 8 == 7 and g >= 8:
                    chain0(g // 8 - 1)
            chain0(NTB - 1)

        @block.vector
        def _(e):
            e.memset(ones_b[:], 1.0)
            e.memset(ones_f[:], 1.0)
            e.memset(ones_f1[:], 1.0)
            e.memset(ones_b1[:], 1.0)
            e.memset(onef11[:], 1.0)
            e.memset(wsum1[:], 0.0)
            e.memset(wsum2[:], 0.0)
            e.memset(epsv[:], EPS)
            e.memset(mgv[:], MAGIC)
            e.memset(t30v[:], 1e-30)
            e.wait_ge(SM["dW"], 16)
            e.tensor_copy(eye_b[:], eye_f[:])
            e.drain().then_inc(SM["vCV"], 1)
            def consume_bcast(b):
                sl = slice(b * 512, (b + 1) * 512)
                e.wait_ge(SM["pBC"], 5 * b + 1)
                e.tensor_tensor(T_rk[:, sl], T_rk[:, sl],
                                ps0_bc[b % 2][:], ALU.mult)
                if b < NTBQ:
                    e.drain()
                    e.scalar_tensor_tensor(T_rq[:, sl],
                                           ps0_bc[b % 2][:], 0.125,
                                           T_rq[:, sl], ALU.mult, ALU.mult)
                e.drain().then_inc(SM["vBC"], 1)
                for k in range(4):
                    j = 4 * b + k
                    e.wait_ge(SM["pBC"], 5 * b + 2 + k)
                    e.tensor_copy(r_tm[:, j:j + 1],
                                  ps0_r1[j % 2][:]).then_inc(SM["vBC"], 1)

            for i in range(NTB):
                sl = slice(i * 512, (i + 1) * 512)
                e.wait_ge(SM["pST"], NDC * (i + 1))
                e.tensor_scalar(m_f.ap()[0:1, sl], ps0_m[:], 1.0,
                                None, ALU.mult)
                e.tensor_scalar(s2r_f.ap()[0:1, sl], ps0_v[:], 1.0,
                                None, ALU.mult)
                e.drain().then_inc(SM["vROW"], 1)
                e.tensor_tensor(rowtmp.ap()[0:1, sl], m_f.ap()[0:1, sl],
                                m_f.ap()[0:1, sl], ALU.mult)
                e.drain()
                e.scalar_tensor_tensor(s2r_f.ap()[0:1, sl],
                                       s2r_f.ap()[0:1, sl], float(D),
                                       rowtmp.ap()[0:1, sl],
                                       ALU.mult, ALU.subtract)
                e.drain()
                e.tensor_scalar(m_f.ap()[0:1, sl], m_f.ap()[0:1, sl],
                                1.0 / D, None, ALU.mult)
                e.drain()
                e.tensor_copy(m_row_b.ap()[0:1, sl], m_f.ap()[0:1, sl])
                if i < NTBQ:
                    e.tensor_copy(mq_row_b.ap()[0:1, sl],
                                  m_f.ap()[0:1, sl])
                e.drain().then_inc(SM["vRC0"], 1)
                if i >= 1:
                    consume_bcast(i - 1)
            consume_bcast(NTB - 1)
            for im, wdst in enumerate([wqcs_n, wkcs_n, wvcs_n]):
                e.wait_ge(SM["pBC"], 5 * NTB + im + 1)
                e.tensor_scalar(wdst.ap(), ps0_w[:], -1.0, None, ALU.mult)
                e.drain().then_inc(SM["vWCS"], 1)

        @block.tensor
        def _(e):
            e.wait_ge(SM["vCV"], 1)

            def bcast_r1(b):
                if b >= 2:
                    e.wait_ge(SM["vBC"], 5 * (b - 2) + 1)
                e.wait_ge(SM["aRC0"], b + 1)
                if b == 0:
                    e.wait_ge(SM["dWT"], 16 * 2)
                e.matmul(ps0_bc[b % 2][:], ones_f1[:],
                         s2r_f.ap()[0:1, b * 512:(b + 1) * 512],
                         start=True, stop=True).then_inc(SM["pBC"], 1)
                for k in range(4):
                    j = 4 * b + k
                    if j >= 2:
                        bp, kp = divmod(j - 2, 4)
                        e.wait_ge(SM["vBC"], 5 * bp + 2 + kp)
                    e.matmul(ps0_r1[j % 2][:],
                             s2r_f.ap()[0:1, j * 128:(j + 1) * 128],
                             onef11[:], start=True,
                             stop=True).then_inc(SM["pBC"], 1)

            for i in range(NTB):
                for c in range(NDC):
                    g = i * NDC + c
                    e.wait_ge(SM[X8[g % 8]], 16 * (g // 8 + 1))
                    if c == 0 and i >= 1:
                        e.wait_ge(SM["vROW"], i)
                    e.matmul(ps0_m[:], ones_b[:], xb[g % 8][:, :],
                             start=(c == 0), stop=(c == NDC - 1),
                             skip_group_check=True)
                    e.wait_ge(SM["aSQ0"], g + 1)
                    e.matmul(ps0_v[:], ones_b[:], xsq[g % 8][:, :],
                             start=(c == 0), stop=(c == NDC - 1),
                             skip_group_check=True).then_inc(SM["pST"], 1)
                if i >= 2:
                    bcast_r1(i - 2)
            bcast_r1(NTB - 2)
            bcast_r1(NTB - 1)
            for im, (wsrc, wb, sem) in enumerate(WL):
                e.wait_ge(SM[sem], 16 * NDC)
                if im >= 1:
                    e.wait_ge(SM["vWCS"], im)
                last = None
                for c in range(NDC):
                    for h in range(2):
                        last = e.matmul(
                            ps0_w[:, h * 512:(h + 1) * 512], ones_b[:],
                            wb[:, c * 1024 + h * 512:c * 1024 + (h + 1) * 512],
                            start=(c == 0), stop=(c == NDC - 1),
                            skip_group_check=True)
                last.then_inc(SM["pBC"], 1)

        @block.sync
        def _(e):
            e.wait_ge(SM["dW"], 16)
            e.wait_ge(SM["dWT"], 16 * 2)
            for nm in X8:
                e.wait_ge(SM[nm], 16 * 8)
            for nm in ("dWWq", "dWWk", "dWWv"):
                e.wait_ge(SM[nm], 16 * 8)

        st0.close()
        barrier_all()

        st_V = ExitStack()
        V_sb = mk(st_V, "V_sb", [128, NTT * VBLK], BF16, side="right")
        negG = mk(st_V, "negG", [128, NP], side="right")
        kmax = mk(st_V, "kmax", [1, NP], side="right")
        qmax = mk(st_V, "qmax", [1, NP], side="right")
        grow = mk(st_V, "grow", [1, NP], side="right")
        st1 = ExitStack()
        xr = [mk(st1, f"xr{i}", [128, NDC * 512], BF16) for i in range(2)]
        krot = [mk(st1, f"krot{i}", [128, 512], BF16) for i in range(2)]
        ksq = [mk(st1, f"ksq{i}", [128, 512], BF16) for i in range(2)]
        rt1 = mk(st1, "rt1", [1, 2])

        ps1_k = [mkp(st1, f"ps1_k{i}", [128, 512]) for i in range(2)]
        ps1_v = [mkp(st1, f"ps1_v{i}", [128, 1024]) for i in range(2)]
        ps1_n = [mkp(st1, f"ps1_n{i}", [1, 512]) for i in range(2)]

        PB = [("k", tb) for tb in range(NTB)] + [("q", tb) for tb in range(NTBQ)]
        NPBIDX = len(PB) * NDC

        @block.gpsimd
        def _(e):
            for bi, (kind, tb) in enumerate(PB):
                src = x_fm if kind == "k" else x_fmq
                if bi >= 2:
                    e.wait_ge(SM["pXF"], bi - 1)
                for c in range(NDC):
                    e.dma_start(
                        out=xr[bi % 2][:, c * 512:(c + 1) * 512],
                        in_=src[c * 128:(c + 1) * 128, tb * 512:(tb + 1) * 512],
                    ).then_inc(SM["dX1a" if bi % 2 == 0 else "dX1b"], 16)

        @block.tensor
        def _(e):
            for bi, (kind, tb) in enumerate(PB):
                e.wait_ge(SM["dX1a" if bi % 2 == 0 else "dX1b"],
                          16 * NDC * (bi // 2 + 1))
                if kind == "k":
                    for t4 in range(4):
                        j = 4 * tb + t4
                        if j >= 2:
                            e.wait_ge(SM["vVE"], j - 1)
                        for c in range(NDC):
                            for h in range(2):
                                e.matmul(
                                    ps1_v[j % 2][:, h * 512:(h + 1) * 512],
                                    xr[bi % 2][:, c * 512 + t4 * 128:
                                               c * 512 + t4 * 128 + 128],
                                    wv_b[:, c * 1024 + h * 512:
                                         c * 1024 + (h + 1) * 512],
                                    start=(c == 0), stop=False,
                                    skip_group_check=True)
                        last = None
                        for h in range(2):
                            last = e.matmul(
                                ps1_v[j % 2][:, h * 512:(h + 1) * 512],
                                m_row_b.ap()[0:1, tb * 512 + t4 * 128:
                                             tb * 512 + t4 * 128 + 128],
                                wvcs_n.ap()[0:1, h * 512:(h + 1) * 512],
                                start=False, stop=True, skip_group_check=True)
                        last.then_inc(SM["pV"], 1)
                wb = wk_b if kind == "k" else wq_b
                wcs = wkcs_n if kind == "k" else wqcs_n
                mrow = m_row_b if kind == "k" else mq_row_b
                for mdo in range(NDC):
                    idx = bi * NDC + mdo
                    if idx >= 2:
                        e.wait_ge(SM["vKR"], idx - 1)
                    for c in range(NDC):
                        mm = e.matmul(
                            ps1_k[idx % 2][:],
                            wb[:, c * 1024 + mdo * 128:c * 1024 + mdo * 128 + 128],
                            xr[bi % 2][:, c * 512:(c + 1) * 512],
                            start=(c == 0), stop=False, skip_group_check=True)
                        if mdo == NDC - 1 and c == NDC - 1:
                            mm.then_inc(SM["pXF"], 1)
                    e.matmul(ps1_k[idx % 2][:],
                             wcs.ap()[0:1, mdo * 128:(mdo + 1) * 128],
                             mrow.ap()[0:1, tb * 512:(tb + 1) * 512],
                             start=False, stop=True,
                             skip_group_check=True).then_inc(SM["pK"], 1)
                    if idx >= 1:
                        e.wait_ge(SM["aKS"], idx)
                        if idx - 1 >= 2:
                            e.wait_ge(SM["vNF"], idx - 2)
                        e.matmul(ps1_n[(idx - 1) % 2][:], ones_b[:],
                                 ksq[(idx - 1) % 2][:], start=True,
                                 stop=True).then_inc(SM["pN"], 1)
            idx = NPBIDX - 1
            e.wait_ge(SM["aKS"], idx + 1)
            e.wait_ge(SM["vNF"], idx - 1)
            e.matmul(ps1_n[idx % 2][:], ones_b[:], ksq[idx % 2][:],
                     start=True, stop=True).then_inc(SM["pN"], 1)

        @block.vector
        def _(e):
            e.memset(kmax[:], 0.0)
            e.memset(qmax[:], 0.0)
            e.memset(
                V_sb.ap().rearrange("p (j2 h w) -> p j2 h w", h=16, w=VW)
                [:, :, :, 64:65], 1.0)
            e.drain()
            for bi, (kind, tb) in enumerate(PB):
                if kind == "k":
                    for t4 in range(4):
                        j = 4 * tb + t4
                        e.wait_ge(SM["pV"], j + 1)
                        vv = V_sb.ap().rearrange(
                            "p (j2 h w) -> p j2 h w", h=16, w=VW)
                        e.tensor_scalar(
                            vv[:, j, :, 0:64],
                            ps1_v[j % 2].ap().rearrange(
                                "p (h w) -> p h w", w=64),
                            r_tm[:, j:j + 1], None,
                            ALU.mult).then_inc(SM["vVE"], 1)
                ttab = T_rk if kind == "k" else T_rq
                for mdo in range(NDC):
                    idx = bi * NDC + mdo
                    e.wait_ge(SM["pK"], idx + 1)
                    if idx >= 2:
                        e.wait_ge(SM["aKS"], idx - 1)
                        e.wait_ge(SM["dSPa" if idx % 2 == 0 else "dSPb"],
                                  16 * (idx // 2))
                    e.tensor_tensor(krot[idx % 2][:], ps1_k[idx % 2][:],
                                    ttab[:, tb * 512:(tb + 1) * 512], ALU.mult)
                    e.drain().then_inc(SM["vKR"], 1)
                    if idx >= 2:
                        pidx = idx - 2
                        pkind = "k" if pidx < NTB * NDC else "q"
                        pm = pidx % NDC
                        e.wait_ge(SM["pN"], pidx + 1)
                        e.reduce_max(rt1.ap()[0:1, 0:1], ps1_n[pidx % 2][:],
                                     axis=AX)
                        e.drain()
                        tgt = kmax if pkind == "k" else qmax
                        e.tensor_tensor(tgt.ap()[0:1, pm:pm + 1],
                                        tgt.ap()[0:1, pm:pm + 1],
                                        rt1.ap()[0:1, 0:1], ALU.max)
                        e.drain().then_inc(SM["vNF"], 1)
            for pidx in (NPBIDX - 2, NPBIDX - 1):
                pkind = "k" if pidx < NTB * NDC else "q"
                pm = pidx % NDC
                e.wait_ge(SM["pN"], pidx + 1)
                e.reduce_max(rt1.ap()[0:1, 0:1], ps1_n[pidx % 2][:], axis=AX)
                e.drain()
                tgt = kmax if pkind == "k" else qmax
                e.tensor_tensor(tgt.ap()[0:1, pm:pm + 1],
                                tgt.ap()[0:1, pm:pm + 1],
                                rt1.ap()[0:1, 0:1], ALU.max)
                e.drain().then_inc(SM["vNF"], 1)

        @block.scalar
        def _(e):
            for idx in range(NPBIDX):
                e.wait_ge(SM["vKR"], idx + 1)
                if idx >= 2:
                    e.wait_ge(SM["pN"], idx - 1)
                e.activation(ksq[idx % 2][:], krot[idx % 2][:], AF.Square)
                e.drain().then_inc(SM["aKS"], 1)

        @block.sync
        def _(e):
            for bi, (kind, tb) in enumerate(PB):
                scr = kscr if kind == "k" else qscr
                for mdo in range(NDC):
                    idx = bi * NDC + mdo
                    e.wait_ge(SM["vKR"], idx + 1)
                    e.dma_start(out=scr[mdo, :, tb * 512:(tb + 1) * 512],
                                in_=krot[idx % 2][:]
                                ).then_inc(SM["dSPa" if idx % 2 == 0 else "dSPb"], 16)

        @block.sync
        def _(e):
            e.wait_ge(SM["dX1a"], 128 * 6)
            e.wait_ge(SM["dX1b"], 128 * 6)
            e.wait_ge(SM["dSPa"], 16 * 48)
            e.wait_ge(SM["dSPb"], 16 * 48)

        st1.close()
        barrier_all()
        st_A.close()

        w1q = mk(st_all, "w1q", [128, NDC * FF], BF16)
        st_C = ExitStack()
        ctx_fm = mk(st_C, "ctx_fm", [128, NP * TQ], BF16)

        st2 = ExitStack()
        Kp = [mk(st2, f"Kp{i}", [128, S], BF16) for i in range(2)]
        Qp = [mk(st2, f"Qp{i}", [128, TQ], BF16) for i in range(2)]
        Et = [mk(st2, f"Et{i}", [128, 1024], BF16) for i in range(4)]
        rlA = mk(st2, "rlA", [1, 512], BF16)
        rlB = mk(st2, "rlB", [1, 512], BF16)
        rl_ln = mk(st2, "rl_ln", [1, 512])
        rlbc = mk(st2, "rlbc", [64, 512])
        w1st_ = mk(st2, "w1st_", [128, 1024])
        w1st = [w1st_, w1st_]

        ps2_S = [mkp(st2, f"ps2_S{i}", [128, 1024]) for i in range(2)]
        ps2_cA = mkp(st2, "ps2_cA", [VW, 512])
        ps2_cB = mkp(st2, "ps2_cB", [VW, 512])
        ps2_rl = mkp(st2, "ps2_rl", [64, 512])
        ps2_m = mkp(st2, "ps2_m", [128, 8])

        NBLK = NP * NTBQ
        NIT = NBLK * NTT

        def it_of(m, qh, j):
            return (m * NTBQ + qh) * NTT + j

        @block.vector
        def _(e):
            e.tensor_tensor(grow.ap(), kmax.ap(), qmax.ap(), ALU.mult)
            e.drain().then_inc(SM["vMS"], 1)

        @block.scalar
        def _(e):
            e.wait_ge(SM["vMS"], 1)
            e.activation(grow.ap(), grow.ap(), AF.Ln, bias=t30v[0:1, 0:1])
            e.drain()
            e.activation(grow.ap(), grow.ap(), AF.Exp, scale=0.5)
            e.drain().then_inc(SM["aEX"], 1)

        @block.vector
        def _(e):
            e.wait_ge(SM["aEX"], 1)
            e.tensor_scalar(grow.ap(), grow.ap(), -1.0, None, ALU.mult)
            e.drain().then_inc(SM["vMS"], 1)

        @block.tensor
        def _(e):
            e.wait_ge(SM["vMS"], 2)
            last = None
            for m in range(NP):
                last = e.matmul(ps2_m[:, m:m + 1], ones_f1[:],
                                grow.ap()[0:1, m:m + 1], start=True, stop=True,
                                skip_group_check=True)
            last.then_inc(SM["pMS"], 1)

        @block.vector
        def _(e):
            e.wait_ge(SM["pMS"], 1)
            e.tensor_copy(negG[:], ps2_m[:])
            e.drain().then_inc(SM["vMS"], 1)

        @block.sync
        def _(e):
            for m in range(NP):
                if m >= 2:
                    e.wait_ge(SM["pSC"], (m - 1) * NTBQ * NTT)
                dkq = SM["dKQa" if m % 2 == 0 else "dKQb"]
                e.dma_start(out=Kp[m % 2][:], in_=kscr[m]).then_inc(dkq, 16)
                e.dma_start(out=Qp[m % 2][:], in_=qscr[m]).then_inc(dkq, 16)

        @block.gpsimd
        def _(e):
            for g in range(64):
                if g >= 1:
                    e.wait_ge(SM["vW1"], g)
                ch = g % 32
                c, hh = ch // 4, ch % 4
                e.dma_start(
                    out=w1st[g % 2][:],
                    in_=w1T[c * 128:(c + 1) * 128,
                            hh * 1024:(hh + 1) * 1024],
                ).then_inc(SM["dW1a" if g % 2 == 0 else "dW1b"], 16)

        @block.tensor
        def _(e):
            e.wait_ge(SM["vMS"], 3)
            for m in range(NP):
                for qh in range(NTBQ):
                    blk = m * NTBQ + qh
                    if blk == 17:
                        e.wait_ge(SM["vW1"], 32)
                        e.matmul(ps2_m[0:1, 0:1], ones_f[:], wsum1[:],
                                 start=True, stop=True,
                                 skip_group_check=True).then_inc(SM["pMS"], 1)
                        e.wait_ge(SM["vMS"], 4)
                        e.matmul(ps2_m[:, 1:2], ones_f1[:],
                                 a1row.ap()[0:1, 0:1], start=True, stop=True,
                                 skip_group_check=True).then_inc(SM["pMS"], 1)
                    def do_av(jj):
                        itp = it_of(m, qh, jj)
                        if jj == 0 and blk >= 1:
                            e.wait_ge(SM["vEV"], blk)
                        e.matmul(ps2_cA[:],
                                 V_sb[:, jj * VBLK + (2 * m) * VW:
                                      jj * VBLK + (2 * m) * VW + VW],
                                 Et[itp % 4][:, 0:512],
                                 start=(jj == 0), stop=(jj == NTT - 1),
                                 skip_group_check=True)
                        e.matmul(ps2_cB[:],
                                 V_sb[:, jj * VBLK + (2 * m + 1) * VW:
                                      jj * VBLK + (2 * m + 1) * VW + VW],
                                 Et[itp % 4][:, 512:1024],
                                 start=(jj == 0), stop=(jj == NTT - 1),
                                 skip_group_check=True).then_inc(SM["pAV"], 1)

                    def rl_mm_a(pblk):
                        e.wait_ge(SM["vRC"], 2 * pblk + 1)
                        if pblk >= 1:
                            e.wait_ge(SM["vRL"], 2 * pblk)
                        e.matmul(ps2_rl[:], ones_b1.ap()[0:1, 0:64], rlA.ap(),
                                 start=True, stop=True,
                                 skip_group_check=True).then_inc(SM["pB2"], 1)

                    def rl_mm_b(pblk):
                        e.wait_ge(SM["vRC"], 2 * pblk + 2)
                        e.wait_ge(SM["vRL"], 2 * pblk + 1)
                        e.matmul(ps2_rl[:], ones_b1.ap()[0:1, 0:64], rlB.ap(),
                                 start=True, stop=True,
                                 skip_group_check=True).then_inc(SM["pB2"], 1)

                    for j in range(NTT):
                        it = it_of(m, qh, j)
                        if qh == 0 and j == 0:
                            e.wait_ge(SM["dKQa" if m % 2 == 0 else "dKQb"],
                                      32 * (m // 2 + 1))
                        if it >= 2:
                            e.wait_ge(SM["aEX"], it)
                        e.matmul(ps2_S[it % 2][:, 0:512],
                                 Kp[m % 2][0:64, j * 128:(j + 1) * 128],
                                 Qp[m % 2][0:64, qh * 512:(qh + 1) * 512],
                                 start=True, stop=True, tile_position=(0, 0),
                                 skip_group_check=True)
                        e.matmul(ps2_S[it % 2][:, 512:1024],
                                 Kp[m % 2][64:128, j * 128:(j + 1) * 128],
                                 Qp[m % 2][64:128, qh * 512:(qh + 1) * 512],
                                 start=True, stop=True, tile_position=(64, 0),
                                 skip_group_check=True).then_inc(SM["pSC"], 1)
                        if j >= 3:
                            do_av(j - 3)
                        if blk >= 1 and j == 1:
                            rl_mm_a(blk - 1)
                        if blk >= 1 and j == 2:
                            rl_mm_b(blk - 1)
                    for jj in (NTT - 3, NTT - 2, NTT - 1):
                        e.wait_ge(SM["aEX"], it_of(m, qh, jj) + 2)
                        do_av(jj)
            rl_mm_a(NBLK - 1)
            rl_mm_b(NBLK - 1)

        @block.scalar
        def _(e):
            e.wait_ge(SM["vMS"], 3)
            for it in range(NIT):
                m = it // (NTBQ * NTT)
                e.wait_ge(SM["pSC"], it + 1)
                if it >= 4:
                    e.wait_ge(SM["pAV"], it - 3)
                e.activation(Et[it % 4][:], ps2_S[it % 2][:], AF.Exp,
                             bias=negG[:, m:m + 1],
                             scale=1.0).then_inc(SM["aEX"], 1)
                if it % NTT == NTT - 1:
                    blk = it // NTT
                    e.wait_ge(SM["pAV"], NTT * (blk + 1))
                    e.activation(rl_ln.ap(), ps2_cA[64:65, :], AF.Ln)
                    e.drain()
                    e.activation(rlA.ap(), rl_ln.ap(), AF.Exp, scale=-1.0)
                    e.drain().then_inc(SM["vRC"], 1)
                    e.activation(rl_ln.ap(), ps2_cB[64:65, :], AF.Ln)
                    e.drain()
                    e.activation(rlB.ap(), rl_ln.ap(), AF.Exp,
                                 scale=-1.0).then_inc(SM["vRC"], 1)

        @block.vector
        def _(e):
            w1p1 = [0]
            w1p2 = [0]

            def pass1_chunk():
                g = w1p1[0]
                if g >= 32:
                    return
                e.wait_ge(SM["dW1a" if g % 2 == 0 else "dW1b"],
                          16 * (g // 2 + 1))
                e.reduce_sum(wtmp1[:], w1st[g % 2][:], axis=AX,
                             apply_absolute_value=True)
                e.drain()
                e.tensor_tensor(wsum1[:], wsum1[:], wtmp1[:], ALU.add)
                e.drain().then_inc(SM["vW1"], 1)
                w1p1[0] = g + 1

            def pass2_chunk():
                ch = w1p2[0]
                if ch >= 32:
                    return
                g = 32 + ch
                c, hh = ch // 4, ch % 4
                e.wait_ge(SM["dW1a" if g % 2 == 0 else "dW1b"],
                          16 * (g // 2 + 1))
                e.tensor_scalar(w1st[g % 2][:], w1st[g % 2][:], inva1[:],
                                MAGIC, ALU.mult, ALU.add)
                e.drain()
                e.tensor_scalar(w1st[g % 2][:], w1st[g % 2][:], MAGIC, 1.0,
                                ALU.subtract, ALU.min)
                e.drain()
                e.tensor_scalar(w1q[:, c * FF + hh * 1024:
                                    c * FF + (hh + 1) * 1024],
                                w1st[g % 2][:], -1.0, None,
                                ALU.max).then_inc(SM["vW1"], 1)
                w1p2[0] = ch + 1

            for blk in range(NBLK):
                m, qh = blk // NTBQ, blk % NTBQ
                if blk == 16:
                    e.wait_ge(SM["pMS"], 2)
                    e.tensor_scalar(a1row.ap()[0:1, 0:1], ps2_m[0:1, 0:1],
                                    1.0 / (D * FF), None, ALU.mult)
                    e.drain()
                    e.tensor_scalar(a1row.ap()[0:1, 0:1], a1row.ap()[0:1, 0:1],
                                    1e-5, None, ALU.max)
                    e.drain().then_inc(SM["vMS"], 1)
                    e.wait_ge(SM["pMS"], 3)
                    e.tensor_scalar(a1c[:], ps2_m[:, 1:2], 1.0 / 127.0, None,
                                    ALU.mult)
                    e.drain()
                    e.reciprocal(inva1[:], ps2_m[:, 1:2])
                    e.drain().then_inc(SM["vMS"], 1)
                e.wait_ge(SM["pB2"], 2 * blk + 1)
                e.tensor_copy(rlbc[:], ps2_rl[:])
                e.drain().then_inc(SM["vRL"], 1)
                e.tensor_tensor(
                    ctx_fm[0:64, m * TQ + qh * 512:m * TQ + (qh + 1) * 512],
                    ps2_cA[0:64, :], rlbc[:], ALU.mult)
                e.drain()
                e.wait_ge(SM["pB2"], 2 * blk + 2)
                e.tensor_copy(rlbc[:], ps2_rl[:])
                e.drain().then_inc(SM["vRL"], 1)
                e.tensor_tensor(
                    ctx_fm[64:128, m * TQ + qh * 512:m * TQ + (qh + 1) * 512],
                    ps2_cB[0:64, :], rlbc[:], ALU.mult)
                e.drain().then_inc(SM["vEV"], 1)
                if blk < 16:
                    pass1_chunk()
                    pass1_chunk()
                elif blk >= 18:
                    pass2_chunk()
                    pass2_chunk()
                    pass2_chunk()
            while w1p2[0] < 32:
                pass2_chunk()

        @block.sync
        def _(e):
            e.wait_ge(SM["dKQa"], 32 * 4)
            e.wait_ge(SM["dKQb"], 32 * 4)
            e.wait_ge(SM["dW1a"], 16 * 32)
            e.wait_ge(SM["dW1b"], 16 * 32)

        st2.close()
        st_V.close()
        barrier_all()

        st_W2 = ExitStack()
        w2q = mk(st_W2, "w2q", [128, 32 * 1024], BF16, side="right")
        st3 = ExitStack()
        wo_b = mk(st3, "wo_b", [128, NDC * 1024], BF16)
        xq = [mk(st3, f"xq{i}", [128, 1024]) for i in range(2)]
        y1r = [mk(st3, f"y1r{i}", [128, 1024]) for i in range(2)]
        w2st = [mk(st3, f"w2st{i}", [128, 1024]) for i in range(3)]

        s1_t = mk(st3, "s1_t", [128, 16])
        m2_t = mk(st3, "m2_t", [128, 16])
        mx_t = mk(st3, "mx_t", [128, 16])
        sqa_t = mk(st3, "sqa_t", [128, 16])
        sqb_t = mk(st3, "sqb_t", [128, 16])
        v2_t = mk(st3, "v2_t", [128, 16])
        rho2_t16 = mk(st3, "rho2_t16", [128, 16])
        t16a = mk(st3, "t16a", [128, 16])
        t16b = mk(st3, "t16b", [128, 16])
        ps3_y = [mkp(st3, f"ps3_y{i}", [128, 1024]) for i in range(2)]
        ps3_m = mkp(st3, "ps3_m", [128, 2])
        ps3_sq = mkp(st3, "ps3_sq", [128, 512])
        ps3_u = mkp(st3, "ps3_u", [128, 1024])

        @block.gpsimd
        def _(e):
            for c in range(NDC):
                e.dma_start(out=wo_b[:, c * 1024:(c + 1) * 1024],
                            in_=woT[c * 128:(c + 1) * 128, :]
                            ).then_inc(SM["dWO" if c % 2 == 0 else "dWOb"], 16)
            for ch in range(32):
                if ch >= 3:
                    e.wait_ge(SM["vW2"], ch - 2)
                e.dma_start(out=w2st[ch % 3][:],
                            in_=w2T[ch * 128:(ch + 1) * 128, :]
                            ).then_inc(SM[["dW2a", "dW2b", "dW2c"][ch % 3]], 16)
            for ch in range(32):
                if ch < 3:
                    e.wait_ge(SM["vW2"], [31, 32, 30][ch])
                else:
                    e.wait_ge(SM["vW2q"], ch - 2)
                e.dma_start(out=w2st[ch % 3][:],
                            in_=w2T[ch * 128:(ch + 1) * 128, :]
                            ).then_inc(SM[["dW2a", "dW2b", "dW2c"][ch % 3]], 16)

        @block.sync
        def _(e):
            for i in range(NTTQ):
                if i >= 2:
                    e.wait_ge(SM["vY1"], i - 1)
                e.dma_start(out=xq[i % 2][:],
                            in_=x_tmq[i * 128:(i + 1) * 128, :]
                            ).then_inc(SM["dXQa" if i % 2 == 0 else "dXQb"], 16)
                if i >= 1:
                    e.wait_ge(SM["vY1"], i)
                    e.dma_start(out=y1scr[(i - 1) * 128:i * 128, :],
                                in_=y1r[(i - 1) % 2][:]
                                ).then_inc(SM["dY1a" if (i - 1) % 2 == 0 else "dY1b"], 16)
            e.wait_ge(SM["vY1"], NTTQ)
            e.dma_start(out=y1scr[(NTTQ - 1) * 128:NTTQ * 128, :],
                        in_=y1r[(NTTQ - 1) % 2][:]
                        ).then_inc(SM["dY1a" if (NTTQ - 1) % 2 == 0 else "dY1b"], 16)

        @block.vector
        def _(e):
            for i in range(NTTQ):
                e.wait_ge(SM["pY"], i + 1)
                e.wait_ge(SM["dXQa" if i % 2 == 0 else "dXQb"],
                          16 * (i // 2 + 1))
                if i >= 2:
                    e.wait_ge(SM["dY1a" if i % 2 == 0 else "dY1b"],
                              16 * (i // 2))
                    e.wait_ge(SM["aP4a"], i - 1)
                e.tensor_tensor(y1r[i % 2][:], ps3_y[i % 2][:], xq[i % 2][:],
                                ALU.add)
                e.drain().then_inc(SM["vY1"], 1)
                e.reduce_sum(s1_t[:, i:i + 1], y1r[i % 2][:], axis=AX)
                e.drain()
                e.tensor_scalar(m2_t[:, i:i + 1], s1_t[:, i:i + 1], 1.0 / D,
                                None, ALU.mult)
                e.drain()
                e.tensor_scalar(ps3_u[:], y1r[i % 2][:], 1.0, m2_t[:, i:i + 1],
                                ALU.mult, ALU.subtract)
                e.drain()
                e.reduce_max(mx_t[:, i:i + 1], ps3_u[:], axis=AX,
                             apply_absolute_value=True
                             ).then_inc(SM["vP4a"], 1)
                for ch in (2 * i, 2 * i + 1):
                    e.wait_ge(SM[["dW2a", "dW2b", "dW2c"][ch % 3]],
                              16 * (ch // 3 + 1))
                    e.reduce_sum(wtmp1[:], w2st[ch % 3][:], axis=AX,
                                 apply_absolute_value=True)
                    e.drain()
                    e.tensor_tensor(wsum2[:], wsum2[:], wtmp1[:], ALU.add)
                    e.drain().then_inc(SM["vW2"], 1)
            e.wait_ge(SM["aP4a"], NTTQ)
            e.tensor_tensor(t16a[:], sqa_t[:], sqb_t[:], ALU.add)
            e.drain()
            e.tensor_tensor(t16b[:], m2_t[:], m2_t[:], ALU.mult)
            e.drain()
            e.scalar_tensor_tensor(v2_t[:], t16a[:], 1.0 / D, t16b[:],
                                   ALU.mult, ALU.subtract)
            e.drain().then_inc(SM["vP4a"], 1)
            e.wait_ge(SM["aP4a"], NTTQ + 1)
            e.tensor_scalar(t16a[:], v2_t[:], EPS, None, ALU.add)
            e.drain()
            e.reciprocal(t16b[:], t16a[:])
            e.drain()
            e.tensor_tensor(t16a[:], v2_t[:], t16b[:], ALU.mult)
            e.drain().then_inc(SM["vP4a"], 1)
            e.wait_ge(SM["aP4a"], NTTQ + 2)
            e.tensor_tensor(t16a[:], rho2_t16[:], t16b[:], ALU.mult)
            e.drain()
            e.tensor_tensor(t16b[:], t16a[:], mx_t[:], ALU.mult)
            e.drain()
            e.tensor_scalar(t16b[:], t16b[:], 1e-5, None, ALU.max)
            e.drain()
            e.tensor_scalar(c1o_t[:], t16b[:], a1c[:], None, ALU.mult)
            e.drain()
            e.reciprocal(v2_t[:], t16b[:])
            e.drain()
            e.tensor_tensor(s1_t[:], t16a[:], v2_t[:], ALU.mult)
            e.drain()
            e.tensor_scalar(c1q_t[:], s1_t[:], 127.0, None, ALU.mult)
            e.drain()
            e.tensor_tensor(sqa_t[:], m2_t[:], c1q_t[:], ALU.mult)
            e.drain()
            e.tensor_scalar(kmag_t[:], sqa_t[:], -1.0, None, ALU.mult)
            e.drain()
            e.wait_ge(SM["pMS"], 4)
            e.tensor_scalar(a1row.ap()[0:1, 1:2], ps3_m[0:1, 0:1],
                            1.0 / (D * FF), None, ALU.mult)
            e.drain()
            e.tensor_scalar(a1row.ap()[0:1, 1:2], a1row.ap()[0:1, 1:2],
                            1e-5, None, ALU.max)
            e.drain().then_inc(SM["vMS"], 1)
            e.wait_ge(SM["pMS"], 5)
            e.tensor_scalar(a2c[:], ps3_m[:, 1:2], 1.0 / 127.0, None, ALU.mult)
            e.drain()
            e.reciprocal(inva2[:], ps3_m[:, 1:2])
            e.drain().then_inc(SM["vMS"], 1)
            for ch in range(32):
                sl = ch % 3
                e.wait_ge(SM["aW2"], ch + 1)
                e.tensor_scalar(w2st[sl][:], w2st[sl][:], MAGIC, 1.0,
                                ALU.subtract, ALU.min)
                e.drain()
                e.tensor_scalar(w2q[:, ch * 1024:(ch + 1) * 1024],
                                w2st[sl][:], -1.0, None,
                                ALU.max).then_inc(SM["vW2q"], 1)

        @block.scalar
        def _(e):
            for i in range(NTTQ):
                e.wait_ge(SM["vP4a"], i + 1)
                e.activation(ps3_sq[:], y1r[i % 2][:, 0:512], AF.Square,
                             accum_out=sqa_t[:, i:i + 1])
                e.drain()
                e.activation(ps3_sq[:], y1r[i % 2][:, 512:1024], AF.Square,
                             accum_out=sqb_t[:, i:i + 1])
                e.drain().then_inc(SM["aP4a"], 1)
            e.wait_ge(SM["vP4a"], NTTQ + 1)
            e.activation(t16a[:], v2_t[:], AF.Ln, bias=epsv[:])
            e.drain()
            e.activation(rho2_t16[:], t16a[:], AF.Exp, scale=-0.5)
            e.drain().then_inc(SM["aP4a"], 1)
            e.wait_ge(SM["vP4a"], NTTQ + 2)
            e.activation(t16b[:], t16a[:], AF.Ln, bias=epsv[:])
            e.drain()
            e.activation(t16b[:], t16b[:], AF.Exp, scale=-0.5)
            e.drain().then_inc(SM["aP4a"], 1)
            e.wait_ge(SM["vMS"], 7)
            p1cnt = [11, 11, 10]
            for ch in range(32):
                sl = ch % 3
                e.wait_ge(SM[["dW2a", "dW2b", "dW2c"][sl]],
                          16 * (p1cnt[sl] + ch // 3 + 1))
                e.activation(w2st[sl][:], w2st[sl][:], AF.Identity,
                             scale=inva2[:],
                             bias=mgv[:]).then_inc(SM["aW2"], 1)

        @block.tensor
        def _(e):
            e.wait_ge(SM["dWO"], 16 * 4)
            e.wait_ge(SM["dWOb"], 16 * 4)
            for i in range(NTTQ):
                if i >= 2:
                    e.wait_ge(SM["vY1"], i - 1)
                last = None
                for c in range(NDC):
                    for h in range(2):
                        last = e.matmul(
                            ps3_y[i % 2][:, h * 512:(h + 1) * 512],
                            ctx_fm[:, c * TQ + i * 128:c * TQ + i * 128 + 128],
                            wo_b[:, c * 1024 + h * 512:c * 1024 + (h + 1) * 512],
                            start=(c == 0), stop=(c == NDC - 1),
                            skip_group_check=True)
                last.then_inc(SM["pY"], 1)
            e.wait_ge(SM["vW2"], 32)
            e.matmul(ps3_m[0:1, 0:1], ones_f[:], wsum2[:],
                     start=True, stop=True,
                     skip_group_check=True).then_inc(SM["pMS"], 1)
            e.wait_ge(SM["vMS"], 6)
            e.matmul(ps3_m[:, 1:2], ones_f1[:], a1row.ap()[0:1, 1:2],
                     start=True, stop=True,
                     skip_group_check=True).then_inc(SM["pMS"], 1)

        @block.sync
        def _(e):
            e.wait_ge(SM["dWO"], 16 * 4)
            e.wait_ge(SM["dWOb"], 16 * 4)
            e.wait_ge(SM["dXQa"], 16 * 8)
            e.wait_ge(SM["dXQb"], 16 * 8)
            e.wait_ge(SM["dY1a"], 16 * 8)
            e.wait_ge(SM["dY1b"], 16 * 8)
            for nm, tot in (("dW2a", 22), ("dW2b", 22), ("dW2c", 20)):
                e.wait_ge(SM[nm], 16 * tot)

        st3.close()
        st_C.close()
        barrier_all()

        st4 = ExitStack()
        b1_bc = mk(st4, "b1_bc", [128, FF])
        b2_bc = mk(st4, "b2_bc", [128, D])
        hb = [mk(st4, f"hb{i}", [128, 512]) for i in range(2)]
        h_sc = mk(st4, "h_sc", [128, FF])
        uscr = mk(st4, "uscr", [128, 512])
        uscr_b = mk(st4, "uscr_b", [128, 1024], BF16)
        h_scb2 = [mk(st4, f"h_scb2{i}", [128, 1024], BF16)
                  for i in range(2)]
        mx8 = mk(st4, "mx8", [128, 8])
        q1fm = mk(st4, "q1fm", [128, 1024], BF16)
        q2fm = mk(st4, "q2fm", [128, FF], BF16)
        y1t = [mk(st4, f"y1t{i}", [128, 1024]) for i in range(3)]
        outb = [mk(st4, f"outb{i}", [128, 1024]) for i in range(2)]
        scr4b = scr4

        ps4_b1 = [mkp(st4, f"ps4_b1{i}", [128, 512]) for i in range(2)]
        ps4_b2 = mkp(st4, "ps4_b2", [128, 1024])
        ps4_tr = [mkp(st4, f"ps4_tr{i}", [128, 128], BF16) for i in range(3)]
        ps4_sq = mkp(st4, "ps4_sq", [128, 512])


        gA_base, gB_base = {}, {}
        _g = 0
        _order = [("A", 0)]
        for _t in range(1, NTTQ):
            _order += [("A", _t), ("B", _t - 1)]
        _order += [("B", NTTQ - 1)]
        for _kind, _t in _order:
            if _kind == "A":
                gA_base[_t] = _g
                _g += NDC
            else:
                gB_base[_t] = _g
                _g += 32

        def gq1(t, i):
            return gA_base[t] + i

        def gq2(t, i):
            return gB_base[t] + i

        GKIND = []
        _ca = _cb = 0
        for _kind, _t in _order:
            if _kind == "A":
                for _i in range(NDC):
                    GKIND.append(("A", _ca))
                    _ca += 1
            else:
                for _i in range(32):
                    GKIND.append(("B", _cb))
                    _cb += 1

        def wait_slot(e, g):
            if g < 3:
                return
            kind, ci = GKIND[g - 3]
            if kind == "A":
                e.wait_ge(SM["vTE"], ci + 1)
            elif ci % 2 == 0:
                e.wait_ge(SM["vTEB"], ci // 2 + 1)
            else:
                e.wait_ge(SM["vTEBD"], ci // 2 + 1)

        @block.gpsimd
        def _(e):
            for t in range(2):
                e.dma_start(out=y1t[t % 3][:],
                            in_=y1scr[t * 128:(t + 1) * 128, :]
                            ).then_inc(SM[["dYLa", "dYLb", "dYLc"][t % 3]], 16)
            for cc in range(5):
                if cc >= 1:
                    e.wait_ge(SM["vC2"], cc)
                src = (b1r[0:1, cc * 1024:(cc + 1) * 1024] if cc < 4
                       else b2r[0:1, :])
                e.dma_start(out=outb[0][0:1, 0:1024],
                            in_=src).then_inc(SM["dB"], 16)
            for t in range(2, NTTQ):
                if t >= 3:
                    e.wait_ge(SM["vOU"], t - 2)
                e.dma_start(out=y1t[t % 3][:],
                            in_=y1scr[t * 128:(t + 1) * 128, :]
                            ).then_inc(SM[["dYLa", "dYLb", "dYLc"][t % 3]], 16)

        @block.sync
        def _(e):
            for t in range(NTTQ):
                e.wait_ge(SM["vOU"], t + 1)
                e.dma_start(out=y_out[t * 128:(t + 1) * 128, :],
                            in_=outb[t % 2][:]
                            ).then_inc(SM["dOUa" if t % 2 == 0 else "dOUb"], 16)
            e.wait_ge(SM["dOUa"], 16 * (NTTQ // 2))
            e.wait_ge(SM["dOUb"], 16 * (NTTQ // 2))

        @block.tensor
        def _(e):
            for cc in range(5):
                e.wait_ge(SM["dB"], 16 * (cc + 1))
                bdst = b1_bc if cc < 4 else b2_bc
                e.matmul(ps4_b1[0][:], ones_f1[:], outb[0][0:1, 0:512],
                         start=True, stop=True, skip_group_check=True)
                e.matmul(ps4_b1[1][:], ones_f1[:], outb[0][0:1, 512:1024],
                         start=True, stop=True,
                         skip_group_check=True).then_inc(SM["pIC"], 1)

        @block.vector
        def _(e):
            for cc in range(5):
                e.wait_ge(SM["pIC"], cc + 1)
                bdst = b1_bc if cc < 4 else b2_bc
                off = cc * 1024 if cc < 4 else 0
                e.tensor_copy(bdst[:, off:off + 512], ps4_b1[0][:])
                e.tensor_copy(bdst[:, off + 512:off + 1024], ps4_b1[1][:])
                e.drain().then_inc(SM["vC2"], 1)

        @block.vector
        def _(e):
            def q1_round(t):
                e.wait_ge(SM[["dYLa", "dYLb", "dYLc"][t % 3]],
                          16 * (t // 3 + 1))
                for hf in range(2):
                    e.tensor_scalar(uscr[:], y1t[t % 3][:, hf * 512:
                                                        (hf + 1) * 512],
                                    c1q_t[:, t:t + 1],
                                    kmag_t[:, t:t + 1], ALU.mult, ALU.add)
                    e.drain()
                    if hf == 0 and t >= 1:
                        e.wait_ge(SM["pTE"], gq1(t - 1, 8))
                    e.tensor_scalar(uscr_b[:, hf * 512:(hf + 1) * 512],
                                    uscr[:], MAGIC, MAGIC,
                                    ALU.add, ALU.subtract)
                    e.drain()
                e.drain().then_inc(SM["vQ1"], 1)
                e.wait_ge(SM["vC2"], 5)
                e.tensor_tensor(y1t[t % 3][:], y1t[t % 3][:], b2_bc[:],
                                ALU.add)
                e.drain()

            def ev1_copies(t):
                for i in range(NDC):
                    e.wait_ge(SM["pTE"], gq1(t, i) + 1)
                    e.tensor_copy(q1fm[:, i * 128:(i + 1) * 128],
                                  ps4_tr[gq1(t, i) % 3][:]
                                  ).then_inc(SM["vTE"], 1)

            def hstt_range(t, fcs):
                for fc in fcs:
                    g = 8 * t + fc
                    e.wait_ge(SM["pL1"], g + 1)
                    if g >= 2:
                        e.wait_ge(SM["aGE"], g - 1)
                    e.scalar_tensor_tensor(hb[g % 2][:], ps4_b1[fc % 2][:],
                                           c1o_t[:, t:t + 1],
                                           b1_bc[:, fc * 512:(fc + 1) * 512],
                                           ALU.mult,
                                           ALU.add).then_inc(SM["vIC"], 1)

            RRS = ["rr", "ssq"]

            def rr_snap(t):
                e.wait_ge(SM["aC2"], t + 1)
                e.tensor_tensor(col(scr4b, RRS[t % 2]), col(scr4b, "v2"),
                                col(scr4b, "rho2"), ALU.mult)
                e.drain()

            def do_out(t):
                e.wait_ge(SM["pL2"], 2 * (t + 1))
                if t >= 2:
                    e.wait_ge(SM["dOUa" if t % 2 == 0 else "dOUb"],
                              16 * (t // 2))
                e.scalar_tensor_tensor(outb[t % 2][:], ps4_b2[:],
                                       col(scr4b, RRS[t % 2]), y1t[t % 3][:],
                                       ALU.mult,
                                       ALU.add).then_inc(SM["vOU"], 1)

            def mx8_one(t, fc):
                e.wait_ge(SM["aGE"], 8 * t + fc + 1)
                e.reduce_max(mx8[:, fc:fc + 1],
                             h_sc[:, fc * 512:(fc + 1) * 512], axis=AX,
                             apply_absolute_value=True)

            q1_round(0)
            ev1_copies(0)
            q1_round(1)
            hstt_range(0, range(0, 2))
            for fc in range(2, NDC):
                hstt_range(0, [fc])
                mx8_one(0, fc - 2)
            for t in range(NTTQ):
                mx8_one(t, 6)
                mx8_one(t, 7)
                e.wait_ge(SM["aSQ"], 8 * (t + 1))
                if t >= 1:
                    e.wait_ge(SM["aC2"], t)
                e.reduce_sum(col(scr4b, "s1"), scr4b[:, 8:16], axis=AX)
                e.drain()
                e.reduce_max(col(scr4b, "mx"), mx8[:], axis=AX)
                e.drain()
                e.tensor_scalar(col(scr4b, "mrc"), col(scr4b, "mx"), 1e-30,
                                None, ALU.max)
                e.drain()
                e.reciprocal(col(scr4b, "t9"), col(scr4b, "mrc"))
                e.drain()
                e.tensor_scalar(col(scr4b, "m2"), col(scr4b, "t9"), 127.0,
                                None, ALU.mult)
                e.drain()
                e.tensor_tensor(col(scr4b, "v2"), col(scr4b, "mrc"), a2c[:],
                                ALU.mult)
                e.drain().then_inc(SM["vSF"], 1)

                def ev2_odd(kk):
                    for i in range(8 * kk + 1, 8 * kk + 8, 2):
                        e.wait_ge(SM["pTE"], gq2(t, i) + 1)
                        e.tensor_copy(q2fm[:, i * 128:(i + 1) * 128],
                                      ps4_tr[gq2(t, i) % 3][:]
                                      ).then_inc(SM["vTEBD"], 1)

                for k in range(4):
                    e.tensor_scalar(h_sc[:, k * 1024:(k + 1) * 1024],
                                    h_sc[:, k * 1024:(k + 1) * 1024],
                                    col(scr4b, "m2"), MAGIC,
                                    ALU.mult, ALU.add)
                    e.drain()
                    q = 4 * t + k
                    if q >= 2:
                        tq, kq = divmod(q - 2, 4)
                        e.wait_ge(SM["pTE"], gB_base[tq] + 8 * kq + 8)
                    e.tensor_scalar(h_scb2[k % 2][:],
                                    h_sc[:, k * 1024:(k + 1) * 1024],
                                    MAGIC, None,
                                    ALU.subtract).then_inc(SM["vQ2c"], 1)
                    if k == 1 and t + 1 < NTTQ:
                        ev1_copies(t + 1)
                    if k >= 1:
                        ev2_odd(k - 1)
                e.drain().then_inc(SM["vQ2"], 1)
                ev2_odd(3)
                if t >= 1:
                    do_out(t - 1)
                if t + 1 < NTTQ:
                    hstt_range(t + 1, range(0, 2))
                    for fc in range(2, NDC):
                        hstt_range(t + 1, [fc])
                        mx8_one(t + 1, fc - 2)
                if t + 2 < NTTQ:
                    q1_round(t + 2)
                rr_snap(t)
            do_out(NTTQ - 1)

        @block.tensor
        def _(e):
            def bl1_range(t, fcs):
                for fc in fcs:
                    if 8 * t + fc >= 2:
                        e.wait_ge(SM["vIC"], 8 * t + fc - 1)
                    last = None
                    for c in range(NDC):
                        last = e.matmul(
                            ps4_b1[fc % 2][:],
                            q1fm[:, c * 128:(c + 1) * 128],
                            w1q[:, c * FF + fc * 512:c * FF + (fc + 1) * 512],
                            start=(c == 0), stop=(c == NDC - 1),
                            skip_group_check=True)
                    last.then_inc(SM["pL1"], 1)

            def part_a1_tr(t):
                e.wait_ge(SM["vQ1"], t + 1)
                for i in range(NDC):
                    g = gq1(t, i)
                    wait_slot(e, g)
                    e.transpose(ps4_tr[g % 3][:],
                                uscr_b[:, i * 128:(i + 1) * 128],
                                eye_b[:]).then_inc(SM["pTE"], 1)

            def part_a1_mm(t):
                e.wait_ge(SM["vTE"], 8 * (t + 1))
                if t == 0:
                    e.wait_ge(SM["vC2"], 5)
                bl1_range(t, range(0, 4))

            def part_a2(t):
                bl1_range(t, range(4, NDC))

            def part_b_tr(t):
                for i in range(32):
                    k = i // 8
                    if i % 8 == 0:
                        e.wait_ge(SM["vQ2c"], 4 * t + k + 1)
                    g = gq2(t, i)
                    wait_slot(e, g)
                    e.transpose(ps4_tr[g % 3][:],
                                h_scb2[k % 2][:, (i % 8) * 128:
                                              (i % 8 + 1) * 128],
                                eye_b[:]).then_inc(SM["pTE"], 1)

            def part_b_mm(t):
                e.wait_ge(SM["vTEB"], 16 * (t + 1))
                e.wait_ge(SM["vTEBD"], 16 * (t + 1))
                if t == 0:
                    e.wait_ge(SM["vW2q"], 32)
                else:
                    e.wait_ge(SM["vOU"], t)
                for h2 in range(2):
                    last = None
                    for fc in range(32):
                        last = e.matmul(
                            ps4_b2[:, h2 * 512:(h2 + 1) * 512],
                            q2fm[:, fc * 128:(fc + 1) * 128],
                            w2q[:, fc * 1024 + h2 * 512:
                                fc * 1024 + (h2 + 1) * 512],
                            start=(fc == 0), stop=(fc == 31),
                            skip_group_check=True)
                    last.then_inc(SM["pL2"], 1)

            part_a1_tr(0)
            part_a1_mm(0)
            part_a2(0)
            for t in range(NTTQ):
                if t + 1 < NTTQ:
                    part_a1_tr(t + 1)
                part_b_tr(t)
                if t + 1 < NTTQ:
                    part_a1_mm(t + 1)
                    part_a2(t + 1)
                part_b_mm(t)

        @block.scalar
        def _(e):
            def chain2_act(t):
                e.wait_ge(SM["vSF"], t + 1)
                e.activation(col(scr4b, "msu"), col(scr4b, "s1"), AF.Ln,
                             scale=1.0 / FF, bias=epsv[:])
                e.drain()
                e.activation(col(scr4b, "rho2"), col(scr4b, "msu"), AF.Exp,
                             scale=-0.5)
                e.drain().then_inc(SM["aC2"], 1)

            for t in range(NTTQ):
                for fc in range(NDC):
                    g = 8 * t + fc
                    e.wait_ge(SM["vIC"], g + 1)
                    if fc == 0 and t >= 1:
                        e.wait_ge(SM["vQ2"], t)
                    e.activation(h_sc[:, fc * 512:(fc + 1) * 512],
                                 hb[g % 2][:], AF.Gelu)
                    e.drain().then_inc(SM["aGE"], 1)
                    e.activation(ps4_sq[:], h_sc[:, fc * 512:(fc + 1) * 512],
                                 AF.Square, accum_out=scr4b[:, 8 + fc:9 + fc])
                    e.drain().then_inc(SM["aSQ"], 1)
                chain2_act(t)
                for i in range(0, 32, 2):
                    e.wait_ge(SM["pTE"], gq2(t, i) + 1)
                    e.activation(q2fm[:, i * 128:(i + 1) * 128],
                                 ps4_tr[gq2(t, i) % 3][:],
                                 AF.Copy).then_inc(SM["vTEB"], 1)

        @block.sync
        def _(e):
            e.wait_ge(SM["dYLa"], 16 * 6)
            e.wait_ge(SM["dYLb"], 16 * 5)
            e.wait_ge(SM["dYLc"], 16 * 5)
            e.wait_ge(SM["dB"], 16 * 5)

        st4.close()
        st_W2.close()
        barrier_all()

    return nc



def _rope_table():
    pos = np.arange(S, dtype=np.float32)
    freqs = np.exp(np.arange(0, 64, 2, dtype=np.float32)
                   * np.float32(-math.log(10000.0) / 64.0)).astype(np.float32)
    ang = pos[:, None] * freqs[None, :]
    cosv = np.cos(ang).astype(np.float32)
    sinv = np.sin(ang).astype(np.float32)
    tab = np.zeros((64, S), dtype=np.float32)
    tab[0::2, :] = cosv.T
    tab[1::2, :] = -sinv.T
    return np.ascontiguousarray(np.concatenate([tab, tab], axis=0))


_NC_CACHE = {}


def kernel(L0_attn_x, L0_attn_Wq, L0_attn_Wk, L0_attn_Wv, L0_attn_Wo,
           L0_ffn_w1, L0_ffn_b1, L0_ffn_w2, L0_ffn_b2):
    x = np.ascontiguousarray(np.asarray(L0_attn_x, dtype=np.float32))
    Wq = np.asarray(L0_attn_Wq, dtype=np.float32)
    Wk = np.asarray(L0_attn_Wk, dtype=np.float32)
    Wv = np.asarray(L0_attn_Wv, dtype=np.float32)
    Wo = np.asarray(L0_attn_Wo, dtype=np.float32)
    w1 = np.asarray(L0_ffn_w1, dtype=np.float32)
    b1 = np.asarray(L0_ffn_b1, dtype=np.float32)
    w2 = np.asarray(L0_ffn_w2, dtype=np.float32)
    b2 = np.asarray(L0_ffn_b2, dtype=np.float32)

    dup = (np.arange(D) // 2) * 2
    wqT_h = np.ascontiguousarray(Wq[dup, :].T)
    wkT_h = np.ascontiguousarray(Wk[dup, :].T)
    wvT_h = np.ascontiguousarray(Wv.T)
    woT_h = np.ascontiguousarray(Wo.T)
    w1T_h = np.ascontiguousarray(w1.T)
    w2T_h = np.ascontiguousarray(w2.T)
    b1r_h = np.ascontiguousarray(b1.reshape(1, FF))
    b2r_h = np.ascontiguousarray(b2.reshape(1, D))
    tab = _rope_table()
    eye = np.eye(128, dtype=np.float32)

    if "nc" not in _NC_CACHE:
        _NC_CACHE["nc"] = build_nc()
    nc = _NC_CACHE["nc"]

    in_maps = []
    for c in range(8):
        b, qh = c // 2, c % 2
        xb_ = x[b]
        in_maps.append({
            "x_fm": np.ascontiguousarray(
                np.concatenate([xb_[qh * TQ:], xb_[:qh * TQ]]).T),
            "x_fmq": np.ascontiguousarray(xb_[qh * TQ:(qh + 1) * TQ].T),
            "x_tmq": np.ascontiguousarray(xb_[qh * TQ:(qh + 1) * TQ]),
            "wqT": wqT_h, "wkT": wkT_h, "wvT": wvT_h, "woT": woT_h,
            "w1T": w1T_h, "w2T": w2T_h, "b1r": b1r_h, "b2r": b2r_h,
            "tfm": np.ascontiguousarray(
                np.concatenate([tab[:, qh * TQ:], tab[:, :qh * TQ]], axis=1)),
            "tfmq": np.ascontiguousarray(tab[:, qh * TQ:(qh + 1) * TQ]),
            "eyef": eye,
        })

    res = run_bass_kernel_spmd(nc, in_maps, core_ids=list(range(8)))
    out = np.empty((B, S, D), dtype=np.float32)
    for c in range(8):
        b, qh = c // 2, c % 2
        out[b, qh * TQ:(qh + 1) * TQ] = res.results[c]["y"]
    return out
